# version 02
# baseline (speedup 1.0000x reference)
import numpy as np
from contextlib import ExitStack

import concourse.bass as bass
import concourse.mybir as mybir
from concourse.bass_utils import run_bass_kernel_spmd

F32 = mybir.dt.float32
BF16 = mybir.dt.bfloat16
I32 = mybir.dt.int32
AF = mybir.ActivationFunctionType
ALU = mybir.AluOpType
AX = mybir.AxisListType

NCORES = 8
TP = 512
NPASS = 2
NQ = 32
NI = 64
NLT = 16
EPS = 1e-5
ARENA = 212816

A_XT = 0
B_LN = 65536
C0 = 98304
D0 = 163840
E0 = 180224
F0 = 196608

C_MI = C0 + 16384
C_MF = C0 + 32768
C_WRAW = [C0 + 0, C0 + 8192]
C_WMSK = [C0 + 16384, C0 + 24576]
C_GL = C0 + 16384
C_GR = C0 + 18432
C_GS = C0 + 20480
C_GM = C0 + 22528
C_RSB = C0 + 24576
C_BMT = C0 + 32768
C_F1 = [C0 + 40960, C0 + 49152]
C_EX = [C0 + 57344, C0 + 61440]

D_SC = [D0 + 0, D0 + 2048, D0 + 4096, D0 + 6144]
D_XBF = [D0 + 8192, D0 + 9216]
D_XSQ = [D0 + 10240, D0 + 11264]
D_MSUM = D0 + 12288
D_UP = [D0 + 0, D0 + 8192]
D_DQ = [D0 + 0, D0 + 4096]
D_FC2 = D0 + 8192
D_Z2 = [D0 + 12288, D0 + 13312]

F_MU = [F0 + 0, F0 + 2048]
F_RSTD = [F0 + 4096, F0 + 6144]
F_OST = [F0 + 8192, F0 + 10240]
F_ONES = F0 + 12288
F_BIAS = F0 + 12544
F_GAM = F0 + 12672
F_BET = F0 + 12800
F_GATE = F0 + 12928
F_BMF = F0 + 13440
F_BMB = F0 + 13952


def _host_layouts(L0_attn_data, L0_attn_mask, L0_attn_weight, L0_attn_bias,
                  L0_ffn_gate_w, L0_ffn_up_w, L0_ffn_down_w, L0_ffn_fc1_w,
                  L0_ffn_fc2_w, L0_ffn_expert_w, L0_ffn_ln_gamma, L0_ffn_ln_beta):
    c = np.ascontiguousarray
    W = np.asarray(L0_attn_weight)
    aw = c(W.reshape(128, 32, 128, 32).transpose(1, 2, 3, 0).reshape(32, 128, 4096))
    up = np.asarray(L0_ffn_up_w)
    upi = c(up.reshape(128, 32, 64, 128).transpose(2, 0, 1, 3).reshape(64, 128, 4096))
    fc1 = np.asarray(L0_ffn_fc1_w)
    fc1i = c(fc1.reshape(128, 32, 16, 128).transpose(2, 0, 1, 3).reshape(16, 128, 4096))
    gate = np.asarray(L0_ffn_gate_w)
    gatei = c(gate.reshape(128, 32 * 8))
    ex = np.asarray(L0_ffn_expert_w)
    exi = c(ex.reshape(16, 128, 16, 128).transpose(2, 1, 0, 3).reshape(16, 128, 2048))
    dn = np.asarray(L0_ffn_down_w)
    dni = c(dn.reshape(64, 128, 128, 32).transpose(3, 1, 0, 2).reshape(32, 128, 8192))
    fc2 = np.asarray(L0_ffn_fc2_w)
    fc2i = c(fc2.reshape(16, 128, 128, 32).transpose(3, 1, 0, 2).reshape(32, 128, 2048))
    bias = c(np.asarray(L0_attn_bias).reshape(128, 32))
    gam = c(np.asarray(L0_ffn_ln_gamma).reshape(128, 32))
    bet = c(np.asarray(L0_ffn_ln_beta).reshape(128, 32))
    mask = np.asarray(L0_attn_mask)
    mb = c(mask.reshape(128, 32, 128, 32).transpose(2, 0, 1, 3).reshape(128, 128, 1024))
    data = np.asarray(L0_attn_data)
    per_core = []
    for cc in range(NCORES):
        A = data[0, 1024 * cc:1024 * (cc + 1), :]
        xi = c(A.reshape(2, 512, 128, 32).transpose(0, 2, 3, 1))
        mslice = c(mb[:, 16 * cc:16 * (cc + 1), :].reshape(128, 16 * 1024))
        per_core.append({
            "x": xi, "aw": aw, "upw": upi, "fc1w": fc1i, "gatew": gatei,
            "exw": exi, "dnw": dni, "fc2w": fc2i, "biasv": bias,
            "gamv": gam, "betv": bet, "maskb": mslice,
        })
    return per_core


class Plan:
    ENGINES = ("tensor", "vector", "scalar", "gpsimd", "sync")

    def __init__(self):
        self.ops = {e: [] for e in self.ENGINES}
        self.val = {}
        self.pend = {e: [] for e in self.ENGINES}

    def cur(self, s):
        return self.val.get(s, 0)

    def emit(self, eng, fn, waits=(), inc=None):
        w = list(self.pend[eng]) + list(waits)
        self.pend[eng] = []
        for s, v in w:
            have = self.val.get(s, 0)
            assert v <= have, f"{eng} waits {s}>={v} but only {have} planned"
        if inc is not None:
            self.val[inc[0]] = self.val.get(inc[0], 0) + inc[1]
        self.ops[eng].append((w, fn, inc))
        return self.val[inc[0]] if inc is not None else None

    def barrier(self):
        snap = [("st", self.cur("st")), ("sv", self.cur("sv")),
                ("ss", self.cur("ss"))]
        own = {"tensor": "st", "vector": "sv", "scalar": "ss"}
        for e in ("tensor", "vector", "scalar"):
            self.pend[e].extend(
                (s, v) for s, v in snap if s != own[e] and v > 0)


def build_nc():
    nc = bass.Bass("TRN2", target_bir_lowering=False)

    x = nc.dram_tensor("x", [2, 128, 32, 512], F32, kind="ExternalInput")
    aw = nc.dram_tensor("aw", [32, 128, 4096], F32, kind="ExternalInput")
    upw = nc.dram_tensor("upw", [64, 128, 4096], F32, kind="ExternalInput")
    fc1w = nc.dram_tensor("fc1w", [16, 128, 4096], F32, kind="ExternalInput")
    gatew = nc.dram_tensor("gatew", [128, 256], F32, kind="ExternalInput")
    exw = nc.dram_tensor("exw", [16, 128, 2048], F32, kind="ExternalInput")
    dnw = nc.dram_tensor("dnw", [32, 128, 8192], F32, kind="ExternalInput")
    fc2w = nc.dram_tensor("fc2w", [32, 128, 2048], F32, kind="ExternalInput")
    biasv = nc.dram_tensor("biasv", [128, 32], F32, kind="ExternalInput")
    gamv = nc.dram_tensor("gamv", [128, 32], F32, kind="ExternalInput")
    betv = nc.dram_tensor("betv", [128, 32], F32, kind="ExternalInput")
    maskb = nc.dram_tensor("maskb", [128, 16 * 1024], I32, kind="ExternalInput")
    out = nc.dram_tensor("out", [2, 128, 32, 512], F32, kind="ExternalOutput")

    cc_in = nc.dram_tensor("cc_in", [128, 16], F32)
    cc_out = nc.dram_tensor("cc_out", [128 * NCORES, 16], F32)
    rs_dram = nc.dram_tensor("rs_dram", [1, 512], F32)

    ctx = ExitStack()
    arena = ctx.enter_context(nc.sbuf_tensor("arena", [128, ARENA], mybir.dt.uint8))
    base = nc.lookup_mloc(arena).addr

    def at(name, shape, dtype, off):
        return nc.alloc_sbuf_tensor_at(name, shape, dtype, offset=base + off)

    xt = at("xt", [128, NQ * 512], F32, A_XT)
    lnT = at("lnT", [128, NQ * 512], BF16, B_LN)
    zbf = at("zbf", [128, NQ * 512], BF16, B_LN)
    hT = at("hT", [128, NI * 512], BF16, C0)
    xlatT = at("xlatT", [128, NLT * 512], BF16, D0)
    moeT = at("moeT", [128, NLT * 512], BF16, E0)
    mi_sb = at("mi_sb", [128, 4096], I32, C_MI)
    mf_sb = at("mf_sb", [128, 4096], F32, C_MF)
    wraw = [at(f"wraw{i}", [128, 4096], BF16, C_WRAW[i]) for i in range(2)]
    wmsk = [at(f"wmsk{i}", [128, 4096], BF16, C_WMSK[i]) for i in range(2)]
    gL = at("gL", [32, 512], F32, C_GL)
    gR = at("gR", [32, 512], F32, C_GR)
    gS = at("gS", [32, 512], F32, C_GS)
    gM = at("gM", [32, 512], F32, C_GM)
    rsum_b = at("rsum_b", [128, 512], F32, C_RSB)
    bmt_t = at("bmt_t", [128, 4096], BF16, C_BMT)
    f1col = [at(f"f1col{i}", [128, 4096], BF16, C_F1[i]) for i in range(2)]
    excol = [at(f"excol{i}", [128, 2048], BF16, C_EX[i]) for i in range(2)]
    sc = [at(f"sc{i}", [128, 512], F32, D_SC[i]) for i in range(4)]
    xbf = [at(f"xbf{i}", [128, 512], BF16, D_XBF[i]) for i in range(2)]
    xsq = [at(f"xsq{i}", [128, 512], BF16, D_XSQ[i]) for i in range(2)]
    msum = at("msum", [128, 16], F32, D_MSUM)
    upcol = [at(f"upcol{i}", [128, 4096], BF16, D_UP[i]) for i in range(2)]
    dqcol = [at(f"dqcol{i}", [128, 2048], BF16, D_DQ[i]) for i in range(2)]
    fc2col = at("fc2col", [128, 2048], BF16, D_FC2)
    z2 = [at(f"z2_{i}", [128, 512], BF16, D_Z2[i]) for i in range(2)]
    mu = [at(f"mu{i}", [128, 512], F32, F_MU[i]) for i in range(2)]
    rstd = [at(f"rstd{i}", [128, 512], F32, F_RSTD[i]) for i in range(2)]
    ost = [at(f"ost{i}", [128, 512], F32, F_OST[i]) for i in range(2)]
    ones = at("ones", [128, 128], BF16, F_ONES)
    bias_sb = at("bias_sb", [128, 32], F32, F_BIAS)
    gam_sb = at("gam_sb", [128, 32], F32, F_GAM)
    bet_sb = at("bet_sb", [128, 32], F32, F_BET)
    gate_sb = at("gate_sb", [128, 256], BF16, F_GATE)
    bmf = at("bmf", [128, 128], F32, F_BMF)
    bmb = at("bmb", [128, 128], BF16, F_BMB)

    pm = [ctx.enter_context(nc.psum_tensor(f"pm{i}", [128, 512], F32)) for i in range(2)]
    pst = [ctx.enter_context(nc.psum_tensor(f"pst{i}", [128, 512], F32)) for i in range(2)]
    pg = ctx.enter_context(nc.psum_tensor("pg", [128, 512], F32))

    sem_names = (["st", "sv", "ss", "sgb", "scc", "sgt", "smk", "sagi", "sagl",
                  "srs", "sx", "scs", "sfc2"]
                 + [f"swr{i}" for i in range(2)] + [f"sf1{i}" for i in range(2)]
                 + [f"sex{i}" for i in range(2)] + [f"sup{i}" for i in range(2)]
                 + [f"sdq{i}" for i in range(2)] + [f"sob{i}" for i in range(2)])
    sems = {n: ctx.enter_context(nc.semaphore(n)) for n in sem_names}

    def xtv(q):
        return xt[:, 512 * q:512 * (q + 1)]

    def lnv(q):
        return lnT[:, 512 * q:512 * (q + 1)]

    def zv(q):
        return zbf[:, 512 * q:512 * (q + 1)]

    def hv(it):
        return hT[:, 512 * it:512 * (it + 1)]

    def xlv(lt):
        return xlatT[:, 512 * lt:512 * (lt + 1)]

    def mov(mt):
        return moeT[:, 512 * mt:512 * (mt + 1)]

    P_ = Plan()

    def T(fn, *waits):
        return P_.emit("tensor", fn, waits, ("st", 1))

    def V(fn, *waits):
        return P_.emit("vector", fn, waits, ("sv", 1))

    def S(fn, *waits):
        return P_.emit("scalar", fn, waits, ("ss", 1))

    def Gc(sem, fn, *waits):
        return P_.emit("gpsimd", fn, waits, (sem, 1))

    def GD(sem, fn, *waits):
        return P_.emit("gpsimd", fn, waits, (sem, 16))

    def YD(sem, fn, *waits):
        return P_.emit("sync", fn, waits, (sem, 16))

    def Yw(*waits):
        P_.emit("sync", None, waits, None)

    Gc("sgb", lambda g: g.memset(ones[:, :], 1.0))
    YD("scs", lambda y: y.dma_start(out=bias_sb[:, :], in_=biasv[:, :]))
    YD("scs", lambda y: y.dma_start(out=gam_sb[:, :], in_=gamv[:, :]))
    YD("scs", lambda y: y.dma_start(out=bet_sb[:, :], in_=betv[:, :]))
    for g_ in range(8):
        YD("sx", lambda y, g=g_: y.dma_start(
            out=xt[:, 2048 * g:2048 * (g + 1)],
            in_=x[0, :, 4 * g:4 * (g + 1), :].rearrange("p a t -> p (a t)")))
    GD("sgt", lambda g: g.dma_start(out=gate_sb[:, :], in_=gatew[:, :]))
    mred = []
    for ch in range(4):
        w = [("sv", mred[ch - 1])] if ch >= 1 else []
        mkv = GD("smk", lambda g, c=ch: g.dma_start(
            out=mi_sb[:, :], in_=maskb[:, 4096 * c:4096 * (c + 1)]), *w)
        V(lambda v: v.tensor_copy(mf_sb[:, :], mi_sb[:, :]), ("smk", mkv))
        mred.append(V(lambda v, c=ch: v.reduce_sum(
            msum[:, 4 * c:4 * (c + 1)],
            mf_sb[:, :].rearrange("p (a b) -> p a b", a=4), axis=AX.X)))
    wrv = {}
    wrv[0] = GD("swr0", lambda g: g.dma_start(out=wraw[0][:, :], in_=aw[0, :, :]))
    wrv[1] = GD("swr1", lambda g: g.dma_start(out=wraw[1][:, :], in_=aw[1, :, :]))
    sagi_v = GD("sagi", lambda g: g.dma_start(out=cc_in[:, :], in_=msum[:, :]),
                ("sv", mred[3]))
    scc_v = Gc("scc", lambda g: g.collective_compute(
        "AllGather", ALU.bypass, replica_groups=[list(range(NCORES))],
        ins=[cc_in.ap().opt()], outs=[cc_out.ap().opt()]), ("sagi", sagi_v))
    sagl_v = GD("sagl", lambda g: g.dma_start(
        out=bmf[:, :], in_=bass.AP(cc_out, 0, [[16, 128], [2048, 8], [1, 16]])),
        ("scc", scc_v))
    bmb_done = V(lambda v: v.tensor_single_scalar(bmb[:, :], bmf[:, :], 0.0,
                                                  ALU.is_gt), ("sagl", sagl_v))
    for r in range(NQ):
        V(lambda v, r=r: v.tensor_copy(bmt_t[:, 128 * r:128 * (r + 1)],
                                       bmb[:, :]))

    down_col = {}
    out_dma = {}
    out_add = {}

    for P in range(NPASS):
        P_.barrier()
        stat = {}
        for q in range(NQ):
            w = []
            if q == 0:
                w.append(("sx", 128 * (P + 1)))
            if q >= 2:
                w.append(("st", stat[q - 2]))
            S(lambda s, q=q: s.activation(xbf[q % 2][:, :], xtv(q), AF.Copy), *w)
            sq_done = S(lambda s, q=q: s.activation(xsq[q % 2][:, :], xtv(q),
                                                    AF.Square))
            w = [("ss", sq_done)]
            if q == 0 and P == 0:
                w.append(("sgb", 1))
            T(lambda t, q=q: t.matmul(pst[0][:, :], ones[:, :], xbf[q % 2][:, :],
                                      start=(q == 0), stop=(q == NQ - 1)), *w)
            stat[q] = T(lambda t, q=q: t.matmul(pst[1][:, :], ones[:, :],
                                                xsq[q % 2][:, :],
                                                start=(q == 0), stop=(q == NQ - 1)))
        V(lambda v: v.tensor_single_scalar(mu[0][:, :], pst[0][:, :],
                                           1.0 / 4096.0, ALU.mult),
          ("st", stat[NQ - 1]))
        V(lambda v: v.tensor_single_scalar(sc[0][:, :], pst[1][:, :],
                                           1.0 / 4096.0, ALU.mult))
        V(lambda v: v.tensor_mul(sc[1][:, :], mu[0][:, :], mu[0][:, :]))
        V(lambda v: v.tensor_sub(sc[2][:, :], sc[0][:, :], sc[1][:, :]))
        varv = V(lambda v: v.tensor_single_scalar(sc[3][:, :], sc[2][:, :],
                                                  EPS, ALU.add))
        rstd1 = S(lambda s: s.activation(rstd[0][:, :], sc[3][:, :], AF.Sqrt),
                  ("sv", varv))
        V(lambda v: v.reciprocal(rstd[0][:, :], rstd[0][:, :]), ("ss", rstd1))
        app = {}
        for q in range(NQ):
            w = []
            V(lambda v, q=q: v.tensor_sub(sc[q % 2][:, :], xtv(q), mu[0][:, :]), *w)
            app[q] = V(lambda v, q=q: v.tensor_mul(lnv(q), sc[q % 2][:, :],
                                                   rstd[0][:, :]))

        P_.barrier()
        if P >= 1:
            for r in range(NQ):
                V(lambda v, r=r: v.tensor_copy(bmt_t[:, 128 * r:128 * (r + 1)],
                                               bmb[:, :]))
        mskd, mmd, evv, evs = {}, {}, {}, {}

        def attn_load(j):
            b = j % 2
            w = []
            if j >= 2:
                w.append(("sv", mskd[j - 2]))
            else:
                w.append(("st", down_col[(P - 1, NQ - 1)]))
            wrv[j] = GD(f"swr{b}", lambda g, j=j: g.dma_start(
                out=wraw[j % 2][:, :], in_=aw[j, :, :]), *w)

        def attn_mask(j):
            w = [(f"swr{j % 2}", wrv[j])]
            if j >= 2:
                w.append(("st", mmd[j - 2]))
            elif j == 0:
                w.append(("sv", bmb_done))
            mskd[j] = V(lambda v, j=j: v.tensor_mul(wmsk[j % 2][:, :],
                                                    wraw[j % 2][:, :],
                                                    bmt_t[:, :]), *w)

        if P >= 1:
            attn_load(0)
        attn_mask(0)
        if P >= 1:
            attn_load(1)
        attn_mask(1)
        for j in range(NQ):
            for q in range(NQ):
                w = []
                if q == 0:
                    w.append(("sv", mskd[j]))
                    if j == 0:
                        w.append(("sv", app[NQ - 1]))
                    if j >= 2:
                        w.append(("sv", evv[j - 2]))
                mmd[j] = T(lambda t, j=j, q=q: t.matmul(
                    pm[j % 2][:, :], wmsk[j % 2][:, 128 * q:128 * (q + 1)],
                    lnv(q), start=(q == 0), stop=(q == NQ - 1)), *w)
            if j + 2 < NQ:
                attn_load(j + 2)
            if j + 1 < NQ:
                attn_mask(j + 1)
            w = [("st", mmd[j])]
            if j >= 2:
                w.append(("ss", evs[j - 2]))
            elif j == 1:
                w.append(("ss", rstd1))
            evv[j] = V(lambda v, j=j: v.tensor_add(sc[2 + j % 2][:, :],
                                                   pm[j % 2][:, :], xtv(j)), *w)
            w = [("sv", evv[j])]
            if j == 0 and P == 0:
                w.append(("scs", 48))
            evs[j] = S(lambda s, j=j: s.activation(
                xtv(j), sc[2 + j % 2][:, :], AF.Identity,
                bias=bias_sb[:, j:j + 1]), *w)

        P_.barrier()
        stat2 = {}
        for q in range(NQ):
            w = []
            if q >= 2:
                w.append(("st", stat2[q - 2]))
            S(lambda s, q=q: s.activation(xbf[q % 2][:, :], xtv(q), AF.Copy), *w)
            sq_done = S(lambda s, q=q: s.activation(xsq[q % 2][:, :], xtv(q),
                                                    AF.Square))
            T(lambda t, q=q: t.matmul(pst[0][:, :], ones[:, :], xbf[q % 2][:, :],
                                      start=(q == 0), stop=(q == NQ - 1)),
              ("ss", sq_done))
            stat2[q] = T(lambda t, q=q: t.matmul(pst[1][:, :], ones[:, :],
                                                 xsq[q % 2][:, :],
                                                 start=(q == 0), stop=(q == NQ - 1)))
        V(lambda v: v.tensor_single_scalar(mu[1][:, :], pst[0][:, :],
                                           1.0 / 4096.0, ALU.mult),
          ("st", stat2[NQ - 1]))
        V(lambda v: v.tensor_single_scalar(sc[0][:, :], pst[1][:, :],
                                           1.0 / 4096.0, ALU.mult))
        V(lambda v: v.tensor_mul(sc[1][:, :], mu[1][:, :], mu[1][:, :]))
        V(lambda v: v.tensor_sub(sc[2][:, :], sc[0][:, :], sc[1][:, :]))
        varv = V(lambda v: v.tensor_single_scalar(sc[3][:, :], sc[2][:, :],
                                                  EPS, ALU.add))
        rstd2 = S(lambda s: s.activation(rstd[1][:, :], sc[3][:, :], AF.Sqrt),
                  ("sv", varv))
        V(lambda v: v.reciprocal(rstd[1][:, :], rstd[1][:, :]), ("ss", rstd2))
        app2 = {}
        for q in range(NQ):
            w = []
            V(lambda v, q=q: v.tensor_sub(sc[q % 2][:, :], xtv(q), mu[1][:, :]), *w)
            app2[q] = V(lambda v, q=q: v.tensor_mul(lnv(q), sc[q % 2][:, :],
                                                    rstd[1][:, :]))

        P_.barrier()
        for q in range(NQ):
            w = [("sgt", 16)] if q == 0 else []
            gmm = T(lambda t, q=q: t.matmul(pg[0:8, :],
                                            gate_sb[:, 8 * q:8 * (q + 1)],
                                            lnv(q), start=(q == 0),
                                            stop=(q == NQ - 1)), *w)
        V(lambda v: v.tensor_copy(gL[:, :], pg[0:32, :]), ("st", gmm))
        rot = lambda r: [(p + r) % 8 if p < 8 else p for p in range(32)]
        V(lambda v: v.stream_shuffle(gR[:, :], gL[:, :], rot(1)))
        V(lambda v: v.tensor_add(gM[:, :], gL[:, :], gR[:, :]))
        for r_ in (2, 3, 4):
            V(lambda v, r=r_: v.stream_shuffle(gR[:, :], gL[:, :], rot(r)))
            V(lambda v: v.tensor_add(gS[:, :], gL[:, :], gR[:, :]))
            V(lambda v: v.tensor_max(gM[:, :], gM[:, :], gS[:, :]))
        for r_ in (4, 2, 1):
            V(lambda v, r=r_: v.stream_shuffle(gR[:, :], gM[:, :], rot(r)))
            top2 = V(lambda v: v.tensor_max(gM[:, :], gM[:, :], gR[:, :]))

        f1v, f1mm, f1ev = {}, {}, {}

        def f1_load(lt):
            b = lt % 2
            w = []
            if lt >= 2:
                w.append(("st", f1mm[lt - 2]))
            elif P == 1:
                w.append(("st", down_col[(P - 1, NQ - 1)]))
            f1v[lt] = GD(f"sf1{b}", lambda g, lt=lt: g.dma_start(
                out=f1col[b][:, :], in_=fc1w[lt, :, :]), *w)

        f1_load(0)
        f1_load(1)
        srs1 = GD("srs", lambda g: g.dma_start(out=rs_dram[:, :], in_=gM[0:1, :]),
                  ("sv", top2))
        rs_ok = GD("srs", lambda g: g.dma_start(
            out=rsum_b[:, :], in_=bass.AP(rs_dram, 0, [[0, 128], [1, 512]])),
            ("srs", srs1), ("st", mmd[NQ - 1]))
        for lt in range(NLT):
            for q in range(NQ):
                w = []
                if q == 0:
                    w.append((f"sf1{lt % 2}", f1v[lt]))
                    if lt >= 2:
                        w.append(("ss", f1ev[lt - 2]))
                    else:
                        w.append(("sv", evv[30 + lt]))
                f1mm[lt] = T(lambda t, lt=lt, q=q: t.matmul(
                    pm[lt % 2][:, :], f1col[lt % 2][:, 128 * q:128 * (q + 1)],
                    lnv(q), start=(q == 0), stop=(q == NQ - 1)), *w)
            if lt + 2 < NLT:
                f1_load(lt + 2)
            w = [("st", f1mm[lt]), ("sv", app2[NQ - 1])]
            if lt == 0 and P == 0:
                w.append(("sagi", 16))
            f1ev[lt] = S(lambda s, lt=lt: s.activation(xlv(lt), pm[lt % 2][:, :],
                                                       AF.Copy), *w)

        P_.barrier()
        exv, exmm, moev = {}, {}, {}

        def ex_load(mt):
            b = mt % 2
            w = []
            if mt >= 2:
                w.append(("st", exmm[mt - 2]))
            elif P == 1:
                w.append(("st", down_col[(P - 1, NQ - 1)]))
            exv[mt] = GD(f"sex{b}", lambda g, mt=mt: g.dma_start(
                out=excol[b][:, :], in_=exw[mt, :, :]), *w)

        ex_load(0)
        ex_load(1)
        for mt in range(NLT):
            for lt in range(NLT):
                w = []
                if lt == 0:
                    w.append((f"sex{mt % 2}", exv[mt]))
                    w.append(("ss", f1ev[NLT - 1]))
                    if mt >= 2:
                        w.append(("sv", moev[mt - 2]))
                exmm[mt] = T(lambda t, mt=mt, lt=lt: t.matmul(
                    pm[mt % 2][:, :], excol[mt % 2][:, 128 * lt:128 * (lt + 1)],
                    xlv(lt), start=(lt == 0), stop=(lt == NLT - 1)), *w)
            if mt + 2 < NLT:
                ex_load(mt + 2)
            moev[mt] = V(lambda v, mt=mt: v.tensor_mul(mov(mt), pm[mt % 2][:, :],
                                                       rsum_b[:, :]),
                         ("st", exmm[mt]), ("srs", rs_ok))

        P_.barrier()
        upv, upmm, uprl, upsq = {}, {}, {}, {}

        def up_load(it):
            b = it % 2
            w = []
            if it >= 2:
                w.append(("st", upmm[it - 2]))
            else:
                w.append(("st", exmm[NLT - 1]))
            upv[it] = GD(f"sup{b}", lambda g, it=it: g.dma_start(
                out=upcol[b][:, :], in_=upw[it, :, :]), *w)

        up_load(0)
        up_load(1)
        for it in range(NI):
            for q in range(NQ):
                w = []
                if q == 0:
                    w.append((f"sup{it % 2}", upv[it]))
                    if it >= 2:
                        w.append(("ss", uprl[it - 2]))
                    else:
                        w.append(("sv", moev[NLT - 2 + it]))
                upmm[it] = T(lambda t, it=it, q=q: t.matmul(
                    pm[it % 2][:, :], upcol[it % 2][:, 128 * q:128 * (q + 1)],
                    lnv(q), start=(q == 0), stop=(q == NQ - 1)), *w)
            if it + 2 < NI:
                up_load(it + 2)
            w = [("st", upmm[it])]
            if P >= 1 and it < 2:
                w.append((f"sob{it % 2}", out_dma[(P - 1, NQ - 2 + it)]))
            uprl[it] = S(lambda s, it=it: s.activation(ost[it % 2][:, :],
                                                       pm[it % 2][:, :], AF.Relu),
                         *w)
            upsq[it] = S(lambda s, it=it: s.activation(hv(it), ost[it % 2][:, :],
                                                       AF.Square))

        P_.barrier()
        dqv, dqmm, fc2v = {}, {}, {}
        zcp, zsq, zst = {}, {}, {}

        def dq_load(kq):
            b = kq % 2
            w = []
            if kq >= 2:
                w.append(("st", dqmm[kq - 2]))
            else:
                w.append(("st", upmm[62]))
            dqv[kq] = GD(f"sdq{b}", lambda g, kq=kq: g.dma_start(
                out=dqcol[b][:, :],
                in_=dnw[kq // 4, :, 2048 * (kq % 4):2048 * (kq % 4 + 1)]), *w)

        def fc2_load(dt):
            w = [("st", down_col[(P, dt - 1)])] if dt >= 1 else \
                [("st", upmm[63])]
            fc2v[dt] = GD("sfc2", lambda g, dt=dt: g.dma_start(
                out=fc2col[:, :], in_=fc2w[dt, :, :]), *w)

        dq_load(0)
        dq_load(1)
        fc2_load(0)
        for dt in range(NQ):
            for u in range(4):
                kq = 4 * dt + u
                for m16 in range(16):
                    it = 16 * u + m16
                    w = []
                    if m16 == 0:
                        w.append((f"sdq{kq % 2}", dqv[kq]))
                        if u == 0:
                            if dt >= 2:
                                w.append(("ss", zsq[dt - 2]))
                            else:
                                w.append(("ss", uprl[62 + dt]))
                    dqmm[kq] = T(lambda t, kq=kq, m16=m16, it=it, dt=dt: t.matmul(
                        pm[dt % 2][:, :],
                        dqcol[kq % 2][:, 128 * m16:128 * (m16 + 1)],
                        hv(it), start=(it == 0), stop=False), *w)
                if kq + 2 < 4 * NQ:
                    dq_load(kq + 2)
            for lt in range(NLT):
                w = []
                if lt == 0:
                    w.append(("sfc2", fc2v[dt]))
                    if dt == 0:
                        w.append(("sv", moev[NLT - 1]))
                down_col[(P, dt)] = T(lambda t, dt=dt, lt=lt: t.matmul(
                    pm[dt % 2][:, :], fc2col[:, 128 * lt:128 * (lt + 1)],
                    mov(lt), start=False, stop=(lt == NLT - 1)), *w)
            if dt + 1 < NQ:
                fc2_load(dt + 1)
            zcp[dt] = S(lambda s, dt=dt: s.activation(zv(dt), pm[dt % 2][:, :],
                                                      AF.Copy),
                        ("st", down_col[(P, dt)]))
            w = [("st", zst[dt - 2])] if dt >= 2 else []
            zsq[dt] = S(lambda s, dt=dt: s.activation(z2[dt % 2][:, :],
                                                      pm[dt % 2][:, :],
                                                      AF.Square), *w)
            if dt >= 1:
                d = dt - 1
                T(lambda t, d=d: t.matmul(pst[0][:, :], ones[:, :], zv(d),
                                          start=(d == 0), stop=(d == NQ - 1)),
                  ("ss", zcp[d]))
                zst[d] = T(lambda t, d=d: t.matmul(pst[1][:, :], ones[:, :],
                                                   z2[d % 2][:, :],
                                                   start=(d == 0),
                                                   stop=(d == NQ - 1)),
                           ("ss", zsq[d]))
        d = NQ - 1
        T(lambda t, d=d: t.matmul(pst[0][:, :], ones[:, :], zv(d),
                                  start=False, stop=True), ("ss", zcp[d]))
        zst[d] = T(lambda t, d=d: t.matmul(pst[1][:, :], ones[:, :],
                                           z2[d % 2][:, :], start=False,
                                           stop=True), ("ss", zsq[d]))
        V(lambda v: v.tensor_single_scalar(mu[0][:, :], pst[0][:, :],
                                           1.0 / 4096.0, ALU.mult),
          ("st", zst[NQ - 1]))
        V(lambda v: v.tensor_single_scalar(sc[0][:, :], pst[1][:, :],
                                           1.0 / 4096.0, ALU.mult))
        V(lambda v: v.tensor_mul(sc[1][:, :], mu[0][:, :], mu[0][:, :]))
        V(lambda v: v.tensor_sub(sc[2][:, :], sc[0][:, :], sc[1][:, :]))
        varv = V(lambda v: v.tensor_single_scalar(sc[3][:, :], sc[2][:, :],
                                                  EPS, ALU.add))
        rstdf = S(lambda s: s.activation(rstd[0][:, :], sc[3][:, :], AF.Sqrt),
                  ("sv", varv))
        V(lambda v: v.reciprocal(rstd[0][:, :], rstd[0][:, :]), ("ss", rstdf))

        P_.barrier()
        vmul, sid = {}, {}
        for dt in range(NQ):
            a, b2 = dt % 2, 2 + dt % 2
            w = [("ss", rstdf)]
            if dt >= 2:
                w.append(("sv", vmul[dt - 2]))
            scp = S(lambda s, dt=dt, a=a: s.activation(sc[a][:, :], zv(dt),
                                                       AF.Copy), *w)
            w = [("ss", scp)]
            if dt >= 2:
                w.append(("ss", sid[dt - 2]))
            V(lambda v, a=a, b2=b2: v.tensor_sub(sc[b2][:, :], sc[a][:, :],
                                                 mu[0][:, :]), *w)
            vmul[dt] = V(lambda v, a=a, b2=b2: v.tensor_mul(sc[a][:, :],
                                                            sc[b2][:, :],
                                                            rstd[0][:, :]))
            sid[dt] = S(lambda s, dt=dt, a=a, b2=b2: s.activation(
                sc[b2][:, :], sc[a][:, :], AF.Identity,
                bias=bet_sb[:, dt:dt + 1], scale=gam_sb[:, dt:dt + 1]),
                ("sv", vmul[dt]))
            w = [("ss", sid[dt])]
            if (P, dt - 2) in out_dma:
                w.append((f"sob{dt % 2}", out_dma[(P, dt - 2)]))
            elif (P - 1, NQ - 2 + dt) in out_dma:
                w.append((f"sob{dt % 2}", out_dma[(P - 1, NQ - 2 + dt)]))
            out_add[(P, dt)] = V(lambda v, dt=dt, b2=b2: v.tensor_add(
                ost[dt % 2][:, :], sc[b2][:, :], xtv(dt)), *w)
            out_dma[(P, dt)] = YD(f"sob{dt % 2}", lambda y, dt=dt, P=P: y.dma_start(
                out=out[P, :, dt, :], in_=ost[dt % 2][:, :]),
                ("sv", out_add[(P, dt)]))
            if P == 0 and dt % 4 == 3:
                g = dt // 4
                YD("sx", lambda y, g=g: y.dma_start(
                    out=xt[:, 2048 * g:2048 * (g + 1)],
                    in_=x[1, :, 4 * g:4 * (g + 1), :].rearrange("p a t -> p (a t)")),
                    ("sv", out_add[(P, dt)]))

    Yw(("sob0", P_.cur("sob0")), ("sob1", P_.cur("sob1")))

    block = ctx.enter_context(nc.Block())

    OWN = {"tensor": "st", "vector": "sv", "scalar": "ss"}

    def replay(engine_name):
        own = OWN.get(engine_name)

        def body(eng):
            seen = {}
            for waits, fn, inc in P_.ops[engine_name]:
                best = {}
                for s, v in waits:
                    if s != own and v > best.get(s, 0):
                        best[s] = v
                for s, v in sorted(best.items()):
                    if v > seen.get(s, 0):
                        eng.wait_ge(sems[s], v)
                        seen[s] = v
                if fn is not None:
                    r = fn(eng)
                    if inc is not None:
                        r.then_inc(sems[inc[0]], inc[1])
        return body

    block.gpsimd(replay("gpsimd"))
    block.sync(replay("sync"))
    block.tensor(replay("tensor"))
    block.vector(replay("vector"))
    block.scalar(replay("scalar"))

    ctx.close()
    return nc


_NC_CACHE = None


def kernel(**inputs):
    global _NC_CACHE
    per_core = _host_layouts(**inputs)
    if _NC_CACHE is None:
        _NC_CACHE = build_nc()
    res = run_bass_kernel_spmd(_NC_CACHE, per_core, core_ids=list(range(NCORES)))
    full = np.empty((8192, 4096), np.float32)
    for c in range(NCORES):
        o = np.asarray(res.results[c]["out"])
        full[1024 * c:1024 * (c + 1)] = o.transpose(0, 3, 1, 2).reshape(1024, 4096)
    return full.reshape(1, 8192, 4096)


# version 03 (final)
# speedup vs baseline: 1.0803x; 1.0803x over previous
import numpy as np
from contextlib import ExitStack

import concourse.bass as bass
import concourse.mybir as mybir
from concourse.bass_utils import run_bass_kernel_spmd

F32 = mybir.dt.float32
BF16 = mybir.dt.bfloat16
I32 = mybir.dt.int32
AF = mybir.ActivationFunctionType
ALU = mybir.AluOpType
AX = mybir.AxisListType

NCORES = 8
TP = 512
NPASS = 2
NQ = 32
NI = 64
NLT = 16
EPS = 1e-5
ARENA = 212816

A_XT = 0
B_LN = 65536
C0 = 98304
D0 = 163840
E0 = 180224
F0 = 196608

C_MI = C0 + 16384
C_MF = C0 + 32768
C_WRAW = [C0 + 0, C0 + 8192, C0 + 40960, C0 + 49152]
C_WMSK = [C0 + 16384, C0 + 24576]
C_GL = C0 + 16384
C_GR = C0 + 18432
C_GS = C0 + 20480
C_GM = C0 + 22528
C_RSB = C0 + 24576
C_BMT = C0 + 32768
C_F1 = [C0 + 40960, C0 + 49152, C0 + 0, C0 + 8192]
C_EX = [C0 + 57344, C0 + 61440]
C_UPP = C0 + 16384

D_SC = [D0 + 0, D0 + 2048, D0 + 4096, D0 + 6144]
D_XBF = [D0 + 8192, D0 + 9216]
D_XSQ = [D0 + 10240, D0 + 11264]
D_MSUM = D0 + 12288
D_ASB = [D0 + 12288, D0 + 14336]
D_UP = [D0 + 0, D0 + 8192]
D_DQ = [D0 + 0, D0 + 4096]
D_FC2 = [D0 + 8192, D0 + 12288]

F_MU = [F0 + 0, F0 + 2048]
F_RSTD = [F0 + 4096, F0 + 6144]
F_OST = [F0 + 8192, F0 + 10240]
F_ONES = F0 + 12288
F_BIAS = F0 + 12544
F_GAM = F0 + 12672
F_BET = F0 + 12800
F_GATE = F0 + 12928
F_BMB = F0 + 13440
F_Z2 = [F0 + 13696, F0 + 14720]
F_BMF = F0 + 13696


def _host_layouts(L0_attn_data, L0_attn_mask, L0_attn_weight, L0_attn_bias,
                  L0_ffn_gate_w, L0_ffn_up_w, L0_ffn_down_w, L0_ffn_fc1_w,
                  L0_ffn_fc2_w, L0_ffn_expert_w, L0_ffn_ln_gamma, L0_ffn_ln_beta):
    c = np.ascontiguousarray
    W = np.asarray(L0_attn_weight)
    aw = c(W.reshape(128, 32, 128, 32).transpose(1, 2, 3, 0).reshape(32, 128, 4096))
    up = np.asarray(L0_ffn_up_w)
    upi = c(up.reshape(128, 32, 64, 128).transpose(2, 0, 1, 3).reshape(64, 128, 4096))
    fc1 = np.asarray(L0_ffn_fc1_w)
    fc1i = c(fc1.reshape(128, 32, 16, 128).transpose(2, 0, 1, 3).reshape(16, 128, 4096))
    gate = np.asarray(L0_ffn_gate_w)
    gatei = c(gate.reshape(128, 32 * 8))
    ex = np.asarray(L0_ffn_expert_w)
    exi = c(ex.reshape(16, 128, 16, 128).transpose(2, 1, 0, 3).reshape(16, 128, 2048))
    dn = np.asarray(L0_ffn_down_w)
    dni = c(dn.reshape(64, 128, 128, 32).transpose(3, 1, 0, 2).reshape(32, 128, 8192))
    fc2 = np.asarray(L0_ffn_fc2_w)
    fc2i = c(fc2.reshape(16, 128, 128, 32).transpose(3, 1, 0, 2).reshape(32, 128, 2048))
    bias = c(np.asarray(L0_attn_bias).reshape(128, 32))
    gam = c(np.asarray(L0_ffn_ln_gamma).reshape(128, 32))
    bet = c(np.asarray(L0_ffn_ln_beta).reshape(128, 32))
    mask = np.asarray(L0_attn_mask)
    mb = c(mask.reshape(128, 32, 128, 32).transpose(2, 0, 1, 3).reshape(128, 128, 1024))
    data = np.asarray(L0_attn_data)
    per_core = []
    for cc in range(NCORES):
        A = data[0, 1024 * cc:1024 * (cc + 1), :]
        xi = c(A.reshape(2, 512, 128, 32).transpose(0, 2, 3, 1))
        mslice = c(mb[:, 16 * cc:16 * (cc + 1), :].reshape(128, 16 * 1024))
        per_core.append({
            "x": xi, "aw": aw, "upw": upi, "fc1w": fc1i, "gatew": gatei,
            "exw": exi, "dnw": dni, "fc2w": fc2i, "biasv": bias,
            "gamv": gam, "betv": bet, "maskb": mslice,
        })
    return per_core


class Plan:
    ENGINES = ("tensor", "vector", "scalar", "gpsimd", "sync")

    def __init__(self):
        self.ops = {e: [] for e in self.ENGINES}
        self.val = {}
        self.pend = {e: [] for e in self.ENGINES}

    def cur(self, s):
        return self.val.get(s, 0)

    def emit(self, eng, fn, waits=(), inc=None):
        w = list(self.pend[eng]) + list(waits)
        self.pend[eng] = []
        for s, v in w:
            have = self.val.get(s, 0)
            assert v <= have, f"{eng} waits {s}>={v} but only {have} planned"
        if inc is not None:
            self.val[inc[0]] = self.val.get(inc[0], 0) + inc[1]
        self.ops[eng].append((w, fn, inc))
        return self.val[inc[0]] if inc is not None else None

    def barrier(self):
        snap = [("st", self.cur("st")), ("sv", self.cur("sv")),
                ("ss", self.cur("ss")), ("sg", self.cur("sg"))]
        own = {"tensor": "st", "vector": "sv", "scalar": "ss",
               "gpsimd": "sg"}
        for e in ("tensor", "vector", "scalar", "gpsimd"):
            self.pend[e].extend(
                (s, v) for s, v in snap if s != own[e] and v > 0)


def build_nc():
    nc = bass.Bass("TRN2", target_bir_lowering=False)

    x = nc.dram_tensor("x", [2, 128, 32, 512], F32, kind="ExternalInput")
    aw = nc.dram_tensor("aw", [32, 128, 4096], F32, kind="ExternalInput")
    upw = nc.dram_tensor("upw", [64, 128, 4096], F32, kind="ExternalInput")
    fc1w = nc.dram_tensor("fc1w", [16, 128, 4096], F32, kind="ExternalInput")
    gatew = nc.dram_tensor("gatew", [128, 256], F32, kind="ExternalInput")
    exw = nc.dram_tensor("exw", [16, 128, 2048], F32, kind="ExternalInput")
    dnw = nc.dram_tensor("dnw", [32, 128, 8192], F32, kind="ExternalInput")
    fc2w = nc.dram_tensor("fc2w", [32, 128, 2048], F32, kind="ExternalInput")
    biasv = nc.dram_tensor("biasv", [128, 32], F32, kind="ExternalInput")
    gamv = nc.dram_tensor("gamv", [128, 32], F32, kind="ExternalInput")
    betv = nc.dram_tensor("betv", [128, 32], F32, kind="ExternalInput")
    maskb = nc.dram_tensor("maskb", [128, 16 * 1024], I32, kind="ExternalInput")
    out = nc.dram_tensor("out", [2, 128, 32, 512], F32, kind="ExternalOutput")

    cc_in = nc.dram_tensor("cc_in", [128, 16], F32)
    cc_out = nc.dram_tensor("cc_out", [128 * NCORES, 16], F32)
    rs_dram = nc.dram_tensor("rs_dram", [1, 512], F32)

    ctx = ExitStack()
    arena = ctx.enter_context(nc.sbuf_tensor("arena", [128, ARENA], mybir.dt.uint8))
    base = nc.lookup_mloc(arena).addr

    def at(name, shape, dtype, off):
        return nc.alloc_sbuf_tensor_at(name, shape, dtype, offset=base + off)

    xt = at("xt", [128, NQ * 512], F32, A_XT)
    lnT = at("lnT", [128, NQ * 512], BF16, B_LN)
    zbf = at("zbf", [128, NQ * 512], BF16, B_LN)
    hT = at("hT", [128, NI * 512], BF16, C0)
    xlatT = at("xlatT", [128, NLT * 512], BF16, D0)
    moeT = at("moeT", [128, NLT * 512], BF16, E0)
    mi_sb = at("mi_sb", [128, 4096], I32, C_MI)
    mf_sb = at("mf_sb", [128, 4096], F32, C_MF)
    wraw = [at(f"wraw{i}", [128, 4096], BF16, C_WRAW[i]) for i in range(4)]
    wmsk = [at(f"wmsk{i}", [128, 4096], BF16, C_WMSK[i]) for i in range(2)]
    gL = at("gL", [32, 512], F32, C_GL)
    gR = at("gR", [32, 512], F32, C_GR)
    gS = at("gS", [32, 512], F32, C_GS)
    gM = at("gM", [32, 512], F32, C_GM)
    rsum_b = at("rsum_b", [128, 512], F32, C_RSB)
    bmt_t = at("bmt_t", [128, 4096], BF16, C_BMT)
    f1col = [at(f"f1col{i}", [128, 4096], BF16, C_F1[i]) for i in range(4)]
    excol = [at(f"excol{i}", [128, 2048], BF16, C_EX[i]) for i in range(2)]
    upp = at("upp", [128, 4096], BF16, C_UPP)
    sc = [at(f"sc{i}", [128, 512], F32, D_SC[i]) for i in range(4)]
    xbf = [at(f"xbf{i}", [128, 512], BF16, D_XBF[i]) for i in range(2)]
    xsq = [at(f"xsq{i}", [128, 512], BF16, D_XSQ[i]) for i in range(2)]
    msum = at("msum", [128, 16], F32, D_MSUM)
    asb = [at(f"asb{i}", [128, 512], F32, D_ASB[i]) for i in range(2)]
    upcol = [at(f"upcol{i}", [128, 4096], BF16, D_UP[i]) for i in range(2)]
    dqcol = [at(f"dqcol{i}", [128, 2048], BF16, D_DQ[i]) for i in range(2)]
    fc2col = [at(f"fc2col{i}", [128, 2048], BF16, D_FC2[i]) for i in range(2)]
    z2 = [at(f"z2_{i}", [128, 512], BF16, F_Z2[i]) for i in range(2)]
    mu = [at(f"mu{i}", [128, 512], F32, F_MU[i]) for i in range(2)]
    rstd = [at(f"rstd{i}", [128, 512], F32, F_RSTD[i]) for i in range(2)]
    ost = [at(f"ost{i}", [128, 512], F32, F_OST[i]) for i in range(2)]
    ones = at("ones", [128, 128], BF16, F_ONES)
    bias_sb = at("bias_sb", [128, 32], F32, F_BIAS)
    gam_sb = at("gam_sb", [128, 32], F32, F_GAM)
    bet_sb = at("bet_sb", [128, 32], F32, F_BET)
    gate_sb = at("gate_sb", [128, 256], BF16, F_GATE)
    bmf = at("bmf", [128, 128], F32, F_BMF)
    bmb = at("bmb", [128, 128], BF16, F_BMB)

    pm = [ctx.enter_context(nc.psum_tensor(f"pm{i}", [128, 512], F32)) for i in range(2)]
    pst = [ctx.enter_context(nc.psum_tensor(f"pst{i}", [128, 512], F32)) for i in range(2)]
    pg = ctx.enter_context(nc.psum_tensor("pg", [128, 512], F32))

    sem_names = (["st", "sv", "ss", "sg", "sgb", "scc", "sgt", "smk", "sagi",
                  "sagl", "srs", "sx", "scs", "supp"]
                 + [f"swr{i}" for i in range(4)] + [f"sf1{i}" for i in range(4)]
                 + [f"sex{i}{h}" for i in range(2) for h in "ab"]
                 + [f"sup{i}{h}" for i in range(2) for h in "ab"]
                 + [f"sdq{i}{h}" for i in range(2) for h in "ab"]
                 + [f"sfc2{i}" for i in range(2)] + [f"sob{i}" for i in range(2)])
    sems = {n: ctx.enter_context(nc.semaphore(n)) for n in sem_names}

    def xtv(q):
        return xt[:, 512 * q:512 * (q + 1)]

    def lnv(q):
        return lnT[:, 512 * q:512 * (q + 1)]

    def zv(q):
        return zbf[:, 512 * q:512 * (q + 1)]

    def hv(it):
        return hT[:, 512 * it:512 * (it + 1)]

    def xlv(lt):
        return xlatT[:, 512 * lt:512 * (lt + 1)]

    def mov(mt):
        return moeT[:, 512 * mt:512 * (mt + 1)]

    P_ = Plan()

    def T(fn, *waits):
        return P_.emit("tensor", fn, waits, ("st", 1))

    def V(fn, *waits):
        return P_.emit("vector", fn, waits, ("sv", 1))

    def S(fn, *waits):
        return P_.emit("scalar", fn, waits, ("ss", 1))

    def G(fn, *waits):
        return P_.emit("gpsimd", fn, waits, ("sg", 1))

    def Gc(sem, fn, *waits):
        return P_.emit("gpsimd", fn, waits, (sem, 1))

    def GD(sem, fn, *waits):
        return P_.emit("gpsimd", fn, waits, (sem, 16))

    def YD(sem, fn, *waits):
        return P_.emit("sync", fn, waits, (sem, 16))

    def Yw(*waits):
        P_.emit("sync", None, waits, None)

    Gc("sgb", lambda g: g.memset(ones[:, :], 1.0))
    YD("scs", lambda y: y.dma_start(out=bias_sb[:, :], in_=biasv[:, :]))
    YD("scs", lambda y: y.dma_start(out=gam_sb[:, :], in_=gamv[:, :]))
    YD("scs", lambda y: y.dma_start(out=bet_sb[:, :], in_=betv[:, :]))
    for g_ in range(8):
        YD("sx", lambda y, g=g_: y.dma_start(
            out=xt[:, 2048 * g:2048 * (g + 1)],
            in_=x[0, :, 4 * g:4 * (g + 1), :].rearrange("p a t -> p (a t)")))
    GD("sgt", lambda g: g.dma_start(out=gate_sb[:, :], in_=gatew[:, :]))
    mred = []
    for ch in range(4):
        w = [("sv", mred[ch - 1])] if ch >= 1 else []
        mkv = GD("smk", lambda g, c=ch: g.dma_start(
            out=mi_sb[:, :], in_=maskb[:, 4096 * c:4096 * (c + 1)]), *w)
        V(lambda v: v.tensor_copy(mf_sb[:, :], mi_sb[:, :]), ("smk", mkv))
        mred.append(V(lambda v, c=ch: v.reduce_sum(
            msum[:, 4 * c:4 * (c + 1)],
            mf_sb[:, :].rearrange("p (a b) -> p a b", a=4), axis=AX.X)))
    wrv = {}
    wrv[0] = GD("swr0", lambda g: g.dma_start(out=wraw[0][:, :], in_=aw[0, :, :]))
    wrv[1] = GD("swr1", lambda g: g.dma_start(out=wraw[1][:, :], in_=aw[1, :, :]))
    wrv[2] = GD("swr2", lambda g: g.dma_start(out=wraw[2][:, :], in_=aw[2, :, :]),
                ("sv", mred[3]))
    wrv[3] = GD("swr3", lambda g: g.dma_start(out=wraw[3][:, :], in_=aw[3, :, :]))
    sagi_v = GD("sagi", lambda g: g.dma_start(out=cc_in[:, :], in_=msum[:, :]),
                ("sv", mred[3]))
    scc_v = Gc("scc", lambda g: g.collective_compute(
        "AllGather", ALU.bypass, replica_groups=[list(range(NCORES))],
        ins=[cc_in.ap().opt()], outs=[cc_out.ap().opt()]), ("sagi", sagi_v))
    sagl_v = GD("sagl", lambda g: g.dma_start(
        out=bmf[:, :], in_=bass.AP(cc_out, 0, [[16, 128], [2048, 8], [1, 16]])),
        ("scc", scc_v))

    down_col = {}
    out_dma = {}
    gl_prev = None
    pst_free = None
    bmb_done = None
    prev_sid = None
    prev_oadd = None

    for P in range(NPASS):
        ix = P % 2
        stat = {}
        for q in range(NQ):
            w = [("sx", 128 * P + 16 * (q // 4 + 1))]
            if q >= 2:
                w.append(("st", stat[q - 2]))
            S(lambda s, q=q: s.activation(xbf[q % 2][:, :], xtv(q), AF.Copy), *w)
            sq_done = S(lambda s, q=q: s.activation(xsq[q % 2][:, :], xtv(q),
                                                    AF.Square))
            w = [("ss", sq_done)]
            if q == 0 and P == 0:
                w.append(("sgb", 1))
            if q == 0 and P >= 1:
                w.append(("sv", pst_free))
            T(lambda t, q=q: t.matmul(pst[0][:, :], ones[:, :], xbf[q % 2][:, :],
                                      start=(q == 0), stop=(q == NQ - 1)), *w)
            stat[q] = T(lambda t, q=q: t.matmul(pst[1][:, :], ones[:, :],
                                                xsq[q % 2][:, :],
                                                start=(q == 0), stop=(q == NQ - 1)))
        V(lambda v, ix=ix: v.tensor_single_scalar(mu[ix][:, :], pst[0][:, :],
                                                  1.0 / 4096.0, ALU.mult),
          ("st", stat[NQ - 1]))
        w = [("ss", prev_sid[30])] if P >= 1 else []
        V(lambda v: v.tensor_single_scalar(sc[0][:, :], pst[1][:, :],
                                           1.0 / 4096.0, ALU.mult), *w)
        w = [("ss", prev_sid[31])] if P >= 1 else []
        V(lambda v, ix=ix: v.tensor_mul(sc[1][:, :], mu[ix][:, :], mu[ix][:, :]),
          *w)
        w = [("sg", prev_oadd[30])] if P >= 1 else []
        V(lambda v: v.tensor_sub(sc[2][:, :], sc[0][:, :], sc[1][:, :]), *w)
        w = [("sg", prev_oadd[31])] if P >= 1 else []
        varv = V(lambda v: v.tensor_single_scalar(sc[3][:, :], sc[2][:, :],
                                                  EPS, ALU.add), *w)
        rstd1 = S(lambda s, ix=ix: s.activation(rstd[ix][:, :], sc[3][:, :],
                                                AF.Sqrt), ("sv", varv))
        V(lambda v, ix=ix: v.reciprocal(rstd[ix][:, :], rstd[ix][:, :]),
          ("ss", rstd1))

        mskd, mmd, evv, evs = {}, {}, {}, {}
        if P == 0:
            bmb_done = V(lambda v: v.tensor_single_scalar(
                bmb[:, :], bmf[:, :], 0.0, ALU.is_gt), ("sagl", sagl_v))
        for r in range(NQ):
            w = []
            if r == 0:
                w = [("st", down_col[(P - 1, NQ - 1)])] if P >= 1 else \
                    [("sv", bmb_done)]
            V(lambda v, r=r: v.tensor_copy(bmt_t[:, 128 * r:128 * (r + 1)],
                                           bmb[:, :]), *w)

        def attn_load(j):
            b = j % 4
            w = []
            if j >= 4:
                w.append(("sv", mskd[j - 4]))
            elif P >= 1:
                w.append(("st", down_col[(P - 1, NQ - 1)]))
            wrv[j] = GD(f"swr{b}", lambda g, j=j: g.dma_start(
                out=wraw[j % 4][:, :], in_=aw[j, :, :]), *w)

        def attn_mask(j):
            w = [(f"swr{j % 4}", wrv[j])]
            if j >= 2:
                w.append(("st", mmd[j - 2]))
            mskd[j] = V(lambda v, j=j: v.tensor_mul(wmsk[j % 2][:, :],
                                                    wraw[j % 4][:, :],
                                                    bmt_t[:, :]), *w)

        attn_mask(0)
        attn_mask(1)

        app = {}
        for q in range(NQ):
            V(lambda v, q=q, ix=ix: v.tensor_sub(asb[q % 2][:, :], xtv(q),
                                                 mu[ix][:, :]),
              ("sx", 128 * P + 16 * (q // 4 + 1)))
            app[q] = V(lambda v, q=q, ix=ix: v.tensor_mul(lnv(q), asb[q % 2][:, :],
                                                          rstd[ix][:, :]))

        for j in range(NQ):
            for q in range(NQ):
                w = []
                if q == 0:
                    w.append(("sv", mskd[j]))
                    if j >= 2:
                        w.append(("sv", evv[j - 2]))
                if j < 2:
                    w.append(("sv", app[q]))
                mmd[j] = T(lambda t, j=j, q=q: t.matmul(
                    pm[j % 2][:, :], wmsk[j % 2][:, 128 * q:128 * (q + 1)],
                    lnv(q), start=(q == 0), stop=(q == NQ - 1)), *w)
            if j + 4 < NQ:
                attn_load(j + 4)
            if 2 <= j + 1 < NQ:
                attn_mask(j + 1)
            w = [("st", mmd[j])]
            if j >= 2:
                w.append(("ss", evs[j - 2]))
            elif j == 1:
                w.append(("ss", rstd1))
            evv[j] = V(lambda v, j=j: v.tensor_add(sc[2 + j % 2][:, :],
                                                   pm[j % 2][:, :], xtv(j)), *w)
            w = [("sv", evv[j])]
            if j == 0 and P == 0:
                w.append(("scs", 48))
            evs[j] = S(lambda s, j=j: s.activation(
                xtv(j), sc[2 + j % 2][:, :], AF.Identity,
                bias=bias_sb[:, j:j + 1]), *w)

        P_.barrier()
        stat2 = {}
        for q in range(NQ):
            w = []
            if q >= 2:
                w.append(("st", stat2[q - 2]))
            S(lambda s, q=q: s.activation(xbf[q % 2][:, :], xtv(q), AF.Copy), *w)
            sq_done = S(lambda s, q=q: s.activation(xsq[q % 2][:, :], xtv(q),
                                                    AF.Square))
            T(lambda t, q=q: t.matmul(pst[0][:, :], ones[:, :], xbf[q % 2][:, :],
                                      start=(q == 0), stop=(q == NQ - 1)),
              ("ss", sq_done))
            stat2[q] = T(lambda t, q=q: t.matmul(pst[1][:, :], ones[:, :],
                                                 xsq[q % 2][:, :],
                                                 start=(q == 0), stop=(q == NQ - 1)))
        V(lambda v: v.tensor_single_scalar(mu[1][:, :], pst[0][:, :],
                                           1.0 / 4096.0, ALU.mult),
          ("st", stat2[NQ - 1]))
        V(lambda v: v.tensor_single_scalar(sc[0][:, :], pst[1][:, :],
                                           1.0 / 4096.0, ALU.mult))
        V(lambda v: v.tensor_mul(sc[1][:, :], mu[1][:, :], mu[1][:, :]))
        V(lambda v: v.tensor_sub(sc[2][:, :], sc[0][:, :], sc[1][:, :]))
        varv = V(lambda v: v.tensor_single_scalar(sc[3][:, :], sc[2][:, :],
                                                  EPS, ALU.add))
        rstd2 = S(lambda s: s.activation(rstd[1][:, :], sc[3][:, :], AF.Sqrt),
                  ("sv", varv))
        V(lambda v: v.reciprocal(rstd[1][:, :], rstd[1][:, :]), ("ss", rstd2))
        app2 = {}
        for q in range(NQ):
            V(lambda v, q=q: v.tensor_sub(asb[q % 2][:, :], xtv(q), mu[1][:, :]))
            app2[q] = V(lambda v, q=q: v.tensor_mul(lnv(q), asb[q % 2][:, :],
                                                    rstd[1][:, :]))

        f1v, f1mm, f1ev = {}, {}, {}

        def f1_load(lt):
            b = lt % 4
            w = []
            if lt >= 4:
                w.append(("st", f1mm[lt - 4]))
            else:
                w.append(("sv", mskd[[30, 31, 28, 29][lt]]))
            f1v[lt] = GD(f"sf1{b}", lambda g, lt=lt: g.dma_start(
                out=f1col[b][:, :], in_=fc1w[lt, :, :]), *w)

        for lt in range(4):
            f1_load(lt)
        exv_a, exv_b, exmm, moev = {}, {}, {}, {}

        def ex_load(mt):
            b = mt % 2
            w = []
            if mt >= 2:
                w.append(("st", exmm[mt - 2]))
            elif P >= 1:
                w.append(("st", down_col[(P - 1, NQ - 1)]))
            exv_a[mt] = GD(f"sex{b}a", lambda g, mt=mt: g.dma_start(
                out=excol[b][:, 0:1024], in_=exw[mt, :, 0:1024]), *w)
            exv_b[mt] = GD(f"sex{b}b", lambda g, mt=mt: g.dma_start(
                out=excol[b][:, 1024:2048], in_=exw[mt, :, 1024:2048]))

        ex_load(0)
        ex_load(1)

        gmm = None
        for q in range(NQ):
            w = [("sv", app2[q])]
            if q == 0:
                w.append(("sgt", 16))
                if P >= 1:
                    w.append(("sv", gl_prev))
            gmm = T(lambda t, q=q: t.matmul(pg[0:8, :],
                                            gate_sb[:, 8 * q:8 * (q + 1)],
                                            lnv(q), start=(q == 0),
                                            stop=(q == NQ - 1)), *w)
            for lt in range(2):
                w2 = []
                if q == 0:
                    w2 = [(f"sf1{lt}", f1v[lt]), ("sv", evv[30 + lt])]
                f1mm[lt] = T(lambda t, lt=lt, q=q: t.matmul(
                    pm[lt % 2][:, :], f1col[lt % 4][:, 128 * q:128 * (q + 1)],
                    lnv(q), start=(q == 0), stop=(q == NQ - 1)), *w2)
        gl_prev = V(lambda v: v.tensor_copy(gL[:, :], pg[0:32, :]), ("st", gmm))
        rot = lambda r: [(p + r) % 8 if p < 8 else p for p in range(32)]
        V(lambda v: v.stream_shuffle(gR[:, :], gL[:, :], rot(1)))
        V(lambda v: v.tensor_add(gM[:, :], gL[:, :], gR[:, :]))
        for r_ in (2, 3, 4):
            V(lambda v, r=r_: v.stream_shuffle(gR[:, :], gL[:, :], rot(r)))
            V(lambda v: v.tensor_add(gS[:, :], gL[:, :], gR[:, :]))
            V(lambda v: v.tensor_max(gM[:, :], gM[:, :], gS[:, :]))
        for r_ in (4, 2, 1):
            V(lambda v, r=r_: v.stream_shuffle(gR[:, :], gM[:, :], rot(r)))
            top2 = V(lambda v: v.tensor_max(gM[:, :], gM[:, :], gR[:, :]))

        for lt in range(2):
            w = [("st", f1mm[lt]), ("sv", app2[NQ - 1])]
            if lt == 0 and P == 0:
                w.append(("sagi", 16))
            f1ev[lt] = S(lambda s, lt=lt: s.activation(xlv(lt), pm[lt % 2][:, :],
                                                       AF.Copy), *w)

        srs1 = GD("srs", lambda g: g.dma_start(out=rs_dram[:, :], in_=gM[0:1, :]),
                  ("sv", top2))
        rs_ok = GD("srs", lambda g: g.dma_start(
            out=rsum_b[:, :], in_=bass.AP(rs_dram, 0, [[0, 128], [1, 512]])),
            ("srs", srs1), ("st", mmd[NQ - 1]))
        uppv = GD("supp", lambda g: g.dma_start(out=upp[:, :], in_=upw[0, :, :]),
                  ("srs", srs1), ("st", gmm))

        for lt in range(2, NLT):
            for q in range(NQ):
                w = []
                if q == 0:
                    w.append((f"sf1{lt % 4}", f1v[lt]))
                    w.append(("ss", f1ev[lt - 2]))
                f1mm[lt] = T(lambda t, lt=lt, q=q: t.matmul(
                    pm[lt % 2][:, :], f1col[lt % 4][:, 128 * q:128 * (q + 1)],
                    lnv(q), start=(q == 0), stop=(q == NQ - 1)), *w)
            if lt + 2 < NLT:
                f1_load(lt + 2)
            w = [("st", f1mm[lt])]
            f1ev[lt] = S(lambda s, lt=lt: s.activation(xlv(lt), pm[lt % 2][:, :],
                                                       AF.Copy), *w)

        P_.barrier()
        for mt in range(NLT):
            for lt in range(NLT):
                w = []
                if lt == 0:
                    w.append((f"sex{mt % 2}a", exv_a[mt]))
                    w.append(("ss", f1ev[NLT - 1]))
                    if mt >= 2:
                        w.append(("sv", moev[mt - 2]))
                if lt == 8:
                    w.append((f"sex{mt % 2}b", exv_b[mt]))
                exmm[mt] = T(lambda t, mt=mt, lt=lt: t.matmul(
                    pm[mt % 2][:, :], excol[mt % 2][:, 128 * lt:128 * (lt + 1)],
                    xlv(lt), start=(lt == 0), stop=(lt == NLT - 1)), *w)
            if mt + 2 < NLT:
                ex_load(mt + 2)
            moev[mt] = V(lambda v, mt=mt: v.tensor_mul(mov(mt), pm[mt % 2][:, :],
                                                       rsum_b[:, :]),
                         ("st", exmm[mt]), ("srs", rs_ok))

        P_.barrier()
        upva, upvb, upmm, uprl, upsq = {}, {}, {}, {}, {}

        def up_load(it):
            b = it % 2
            w = []
            if it >= 3:
                w.append(("st", upmm[it - 2]))
            else:
                w.append(("st", exmm[NLT - 1]))
            upva[it] = GD(f"sup{b}a", lambda g, it=it: g.dma_start(
                out=upcol[b][:, 0:2048], in_=upw[it, :, 0:2048]), *w)
            upvb[it] = GD(f"sup{b}b", lambda g, it=it: g.dma_start(
                out=upcol[b][:, 2048:4096], in_=upw[it, :, 2048:4096]))

        up_load(1)
        up_load(2)
        for it in range(NI):
            stat_src = upp if it == 0 else upcol[it % 2]
            for q in range(NQ):
                w = []
                if q == 0:
                    if it == 0:
                        w.append(("supp", uppv))
                        w.append(("sv", moev[NLT - 2]))
                    else:
                        w.append((f"sup{it % 2}a", upva[it]))
                        if it == 1:
                            w.append(("sv", moev[NLT - 1]))
                        else:
                            w.append(("ss", uprl[it - 2]))
                if q == 16 and it != 0:
                    w.append((f"sup{it % 2}b", upvb[it]))
                upmm[it] = T(lambda t, it=it, q=q, ssrc=stat_src: t.matmul(
                    pm[it % 2][:, :], ssrc[:, 128 * q:128 * (q + 1)],
                    lnv(q), start=(q == 0), stop=(q == NQ - 1)), *w)
            if 1 <= it and it + 2 < NI:
                up_load(it + 2)
            w = [("st", upmm[it])]
            if P >= 1 and it < 2:
                w.append((f"sob{it % 2}", out_dma[(P - 1, NQ - 2 + it)]))
            uprl[it] = S(lambda s, it=it: s.activation(ost[it % 2][:, :],
                                                       pm[it % 2][:, :], AF.Relu),
                         *w)
            upsq[it] = S(lambda s, it=it: s.activation(hv(it), ost[it % 2][:, :],
                                                       AF.Square))

        P_.barrier()
        dqva, dqvb, dqmm, fc2v = {}, {}, {}, {}
        zcp, zsq, zst = {}, {}, {}

        def dq_load(kq):
            b = kq % 2
            w = []
            if kq >= 2:
                w.append(("st", dqmm[kq - 2]))
            else:
                w.append(("st", upmm[62]))
            kb, ko = kq // 4, 2048 * (kq % 4)
            dqva[kq] = GD(f"sdq{b}a", lambda g, kb=kb, ko=ko: g.dma_start(
                out=dqcol[b][:, 0:1024], in_=dnw[kb, :, ko:ko + 1024]), *w)
            dqvb[kq] = GD(f"sdq{b}b", lambda g, kb=kb, ko=ko: g.dma_start(
                out=dqcol[b][:, 1024:2048], in_=dnw[kb, :, ko + 1024:ko + 2048]))

        def fc2_load(dt):
            w = [("st", down_col[(P, dt - 2)])] if dt >= 2 else \
                [("st", upmm[63])]
            fc2v[dt] = GD(f"sfc2{dt % 2}", lambda g, dt=dt: g.dma_start(
                out=fc2col[dt % 2][:, :], in_=fc2w[dt, :, :]), *w)

        dq_load(0)
        dq_load(1)
        fc2_load(0)
        fc2_load(1)
        for dt in range(NQ):
            for u in range(4):
                kq = 4 * dt + u
                for m16 in range(16):
                    it = 16 * u + m16
                    w = []
                    if m16 == 0:
                        w.append((f"sdq{kq % 2}a", dqva[kq]))
                        if u == 0:
                            if dt >= 2:
                                w.append(("ss", zsq[dt - 2]))
                            else:
                                w.append(("ss", uprl[62 + dt]))
                    if m16 == 8:
                        w.append((f"sdq{kq % 2}b", dqvb[kq]))
                    dqmm[kq] = T(lambda t, kq=kq, m16=m16, it=it, dt=dt: t.matmul(
                        pm[dt % 2][:, :],
                        dqcol[kq % 2][:, 128 * m16:128 * (m16 + 1)],
                        hv(it), start=(it == 0), stop=False), *w)
                if kq + 2 < 4 * NQ:
                    dq_load(kq + 2)
            for lt in range(NLT):
                w = []
                if lt == 0:
                    w.append((f"sfc2{dt % 2}", fc2v[dt]))
                    if dt == 0:
                        w.append(("sv", moev[NLT - 1]))
                down_col[(P, dt)] = T(lambda t, dt=dt, lt=lt: t.matmul(
                    pm[dt % 2][:, :], fc2col[dt % 2][:, 128 * lt:128 * (lt + 1)],
                    mov(lt), start=False, stop=(lt == NLT - 1)), *w)
            if dt + 2 < NQ:
                fc2_load(dt + 2)
            zcp[dt] = S(lambda s, dt=dt: s.activation(zv(dt), pm[dt % 2][:, :],
                                                      AF.Copy),
                        ("st", down_col[(P, dt)]))
            w = [("st", zst[dt - 2])] if dt >= 2 else []
            zsq[dt] = S(lambda s, dt=dt: s.activation(z2[dt % 2][:, :],
                                                      pm[dt % 2][:, :],
                                                      AF.Square), *w)
            if dt >= 1:
                d = dt - 1
                T(lambda t, d=d: t.matmul(pst[0][:, :], ones[:, :], zv(d),
                                          start=(d == 0), stop=(d == NQ - 1)),
                  ("ss", zcp[d]))
                zst[d] = T(lambda t, d=d: t.matmul(pst[1][:, :], ones[:, :],
                                                   z2[d % 2][:, :],
                                                   start=(d == 0),
                                                   stop=(d == NQ - 1)),
                           ("ss", zsq[d]))
        d = NQ - 1
        T(lambda t, d=d: t.matmul(pst[0][:, :], ones[:, :], zv(d),
                                  start=False, stop=True), ("ss", zcp[d]))
        zst[d] = T(lambda t, d=d: t.matmul(pst[1][:, :], ones[:, :],
                                           z2[d % 2][:, :], start=False,
                                           stop=True), ("ss", zsq[d]))
        V(lambda v: v.tensor_single_scalar(mu[0][:, :], pst[0][:, :],
                                           1.0 / 4096.0, ALU.mult),
          ("st", zst[NQ - 1]))
        pst_free = V(lambda v: v.tensor_single_scalar(sc[0][:, :], pst[1][:, :],
                                                      1.0 / 4096.0, ALU.mult))
        V(lambda v: v.tensor_mul(sc[1][:, :], mu[0][:, :], mu[0][:, :]))
        V(lambda v: v.tensor_sub(sc[2][:, :], sc[0][:, :], sc[1][:, :]))
        varv = V(lambda v: v.tensor_single_scalar(sc[3][:, :], sc[2][:, :],
                                                  EPS, ALU.add))
        rstdf = S(lambda s: s.activation(rstd[0][:, :], sc[3][:, :], AF.Sqrt),
                  ("sv", varv))
        V(lambda v: v.reciprocal(rstd[0][:, :], rstd[0][:, :]), ("ss", rstdf))

        if P + 1 < NPASS:
            for j in range(4):
                wrv[j] = GD(f"swr{j}", lambda g, j=j: g.dma_start(
                    out=wraw[j][:, :], in_=aw[j, :, :]),
                    ("st", down_col[(P, NQ - 1)]))
        vmul, sid, out_add = {}, {}, {}
        for dt in range(NQ):
            a, b2 = dt % 2, 2 + dt % 2
            w = [("ss", rstdf)]
            if dt >= 2:
                w.append(("sg", out_add[dt - 2]))
            vsub = V(lambda v, dt=dt, b2=b2: v.tensor_sub(sc[b2][:, :], zv(dt),
                                                          mu[0][:, :]), *w)
            w = [("ss", sid[dt - 2])] if dt >= 2 else []
            vmul[dt] = V(lambda v, a=a, b2=b2: v.tensor_mul(sc[a][:, :],
                                                            sc[b2][:, :],
                                                            rstd[0][:, :]), *w)
            sid[dt] = S(lambda s, dt=dt, a=a, b2=b2: s.activation(
                sc[b2][:, :], sc[a][:, :], AF.Identity,
                bias=bet_sb[:, dt:dt + 1], scale=gam_sb[:, dt:dt + 1]),
                ("sv", vmul[dt]))
            w = [("ss", sid[dt])]
            if (P, dt - 2) in out_dma:
                w.append((f"sob{dt % 2}", out_dma[(P, dt - 2)]))
            elif (P - 1, NQ - 2 + dt) in out_dma:
                w.append((f"sob{dt % 2}", out_dma[(P - 1, NQ - 2 + dt)]))
            out_add[dt] = G(lambda g, dt=dt, b2=b2: g.tensor_add(
                ost[dt % 2][:, :], sc[b2][:, :], xtv(dt)), *w)
            out_dma[(P, dt)] = YD(f"sob{dt % 2}", lambda y, dt=dt, P=P: y.dma_start(
                out=out[P, :, dt, :], in_=ost[dt % 2][:, :]),
                ("sg", out_add[dt]))
            if P == 0 and dt % 4 == 3:
                g = dt // 4
                YD("sx", lambda y, g=g: y.dma_start(
                    out=xt[:, 2048 * g:2048 * (g + 1)],
                    in_=x[1, :, 4 * g:4 * (g + 1), :].rearrange("p a t -> p (a t)")),
                    ("sg", out_add[dt]))
        prev_sid, prev_oadd = sid, out_add

    Yw(("sob0", P_.cur("sob0")), ("sob1", P_.cur("sob1")))

    block = ctx.enter_context(nc.Block())

    OWN = {"tensor": "st", "vector": "sv", "scalar": "ss", "gpsimd": "sg"}

    def replay(engine_name):
        own = OWN.get(engine_name)

        def body(eng):
            seen = {}
            for waits, fn, inc in P_.ops[engine_name]:
                best = {}
                for s, v in waits:
                    if s != own and v > best.get(s, 0):
                        best[s] = v
                for s, v in sorted(best.items()):
                    if v > seen.get(s, 0):
                        eng.wait_ge(sems[s], v)
                        seen[s] = v
                if fn is not None:
                    r = fn(eng)
                    if inc is not None:
                        r.then_inc(sems[inc[0]], inc[1])
        return body

    block.gpsimd(replay("gpsimd"))
    block.sync(replay("sync"))
    block.tensor(replay("tensor"))
    block.vector(replay("vector"))
    block.scalar(replay("scalar"))

    ctx.close()
    return nc


_NC_CACHE = None


def kernel(**inputs):
    global _NC_CACHE
    per_core = _host_layouts(**inputs)
    if _NC_CACHE is None:
        _NC_CACHE = build_nc()
    res = run_bass_kernel_spmd(_NC_CACHE, per_core, core_ids=list(range(NCORES)))
    full = np.empty((8192, 4096), np.float32)
    for c in range(NCORES):
        o = np.asarray(res.results[c]["out"])
        full[1024 * c:1024 * (c + 1)] = o.transpose(0, 3, 1, 2).reshape(1024, 4096)
    return full.reshape(1, 8192, 4096)
